# version 02 (final)
# baseline (speedup 1.0000x reference)
import numpy as np
import ml_dtypes

import concourse.bass as bass
import concourse.mybir as mybir
from concourse.bass_utils import run_bass_kernel_spmd

F32 = mybir.dt.float32
BF16 = mybir.dt.bfloat16
AF = mybir.ActivationFunctionType
ALU = mybir.AluOpType
AX = mybir.AxisListType

NEG = -1.0e9
BF = ml_dtypes.bfloat16


class Cfg:
    def __init__(self, NB=8192, OWN=2048, ET=13, W=256, n_layers=6, stage=99,
                 debug=(), half=False, phase=99, o1_p=None, o1_kinds="kv",
                 o1_cc=8, o1_ln_only=False, ln_mode="full", o1_tiles=16,
                 tr_n=8):
        self.phase = phase
        self.o1_p = o1_p
        self.o1_kinds = o1_kinds
        self.o1_cc = o1_cc
        self.o1_ln_only = o1_ln_only
        self.ln_mode = ln_mode
        self.o1_tiles = o1_tiles
        self.tr_n = tr_n
        self.B = 2
        self.NB = NB
        self.C = 1024
        self.H = 16
        self.HD = 64
        self.W = W
        self.DI = 2730
        self.OWN = OWN
        self.OT = OWN // 128
        self.ET = ET
        self.EXT = ET * 128
        self.TT = self.OT + self.ET
        self.TOT = self.TT * 128
        self.QTOT = ((self.TOT + 511) // 512) * 512
        self.n_layers = n_layers
        self.stage = min(stage, n_layers)
        self.half = half
        self.debug = debug
        r = self.EXT
        self.os_f, self.os_a = [0] * n_layers, [0] * n_layers
        for i in range(n_layers - 1, -1, -1):
            self.os_f[i] = r // 128
            r -= 1
            self.os_a[i] = r // 128
            r -= (W + 1)
        assert r >= 0, f"extension too small: slack {r}"


def host_rotary(cfg, core):
    q = core % 4
    start = q * cfg.OWN - cfg.EXT
    pos = np.clip(np.arange(start, start + cfg.TOT, dtype=np.float64), 0, None)
    half = np.arange(0, cfg.HD, 2, dtype=np.float64)
    inv_freq = 1.0 / (10000.0 ** (half / cfg.HD))
    fr = pos[:, None] * inv_freq[None, :]
    fr = np.concatenate([fr, fr], axis=1)
    base = (half + 0.4 * cfg.HD) / (1.4 * cfg.HD)
    power = (pos - cfg.NB // 2) / (cfg.W // 2)
    sc = base[None, :] ** power[:, None]
    sc = np.concatenate([sc, sc], axis=1)
    cos, sin = np.cos(fr), np.sin(fr)
    t4 = np.stack([(cos * sc).T, (sin * sc).T, (cos / sc).T, (sin / sc).T])
    return np.concatenate([t4, t4], axis=1).astype(BF)


def host_trimask(cfg):
    qp = np.arange(128)[:, None]
    kf = np.arange(384)[None, :]
    allowed = (kf >= qp) & (kf <= qp + cfg.W)
    return np.where(allowed, 0.0, NEG).astype(np.float32)


def host_rotperm():
    P1 = np.zeros((64, 64), dtype=np.float32)
    for d in range(64):
        if d < 32:
            P1[d + 32, d] = -1.0
        else:
            P1[d - 32, d] = 1.0
    P2 = np.zeros((128, 128), dtype=np.float32)
    P2[:64, :64] = P1
    P2[64:, 64:] = P1
    return P2.astype(BF)


ENGINES = ("tensor", "vector", "scalar", "gpsimd", "sync")


class P:
    def __init__(self, nc):
        self.nc = nc
        self.ops = {e: [] for e in ENGINES}
        self.sems = {}
        self.counts = {}
        self._g = []

    def _sem(self, name):
        if name not in self.sems:
            g = self.nc.semaphore(name)
            self.sems[name] = g.__enter__()
            self._g.append(g)
            self.counts[name] = 0
        return self.sems[name]

    def val(self, name):
        self._sem(name)
        return self.counts[name]

    def on(self, eng, fn):
        self.ops[eng].append(fn)

    def wait(self, eng, name, value):
        if value is None or value <= 0:
            return
        if name[0] in "DS" and not name.startswith("bar_"):
            value = self.counts[name]
        s = self._sem(name)
        self.on(eng, lambda e, s=s, value=value: e.wait_ge(s, value))

    def emit(self, eng, fn, inc=None, by=1):
        if inc is None:
            self.on(eng, fn)
            return None
        s = self._sem(inc)
        self.counts[inc] += by
        v = self.counts[inc]
        self.on(eng, lambda e, fn=fn, s=s, by=by: fn(e).then_inc(s, by))
        return v

    def dma(self, eng, out, in_, inc="LD"):
        return self.emit(eng, lambda e, out=out, in_=in_: e.dma_start(out=out, in_=in_), inc, 16)

    def drain(self, eng):
        self.on(eng, lambda e: e.drain())

    def barrier(self, engines=ENGINES):
        for name, cnt in list(self.counts.items()):
            if name[0] in "DS" and not name.startswith("bar_") and cnt > 0:
                self.wait("sync", name, cnt)
                self.wait("gpsimd", name, cnt)
        marks = {}
        for e in engines:
            s = self._sem("bar_" + e)
            self.counts["bar_" + e] += 1
            marks[e] = self.counts["bar_" + e]
            self.on(e, lambda eng, s=s: eng.nop().then_inc(s, 1))
        for e in engines:
            for o in engines:
                if o != e:
                    self.wait(e, "bar_" + o, marks[o])


def build_program(cfg):
    nc = bass.Bass("TRN2", target_bir_lowering=False, debug=False)
    C, H, HD = cfg.C, cfg.H, cfg.HD
    TT, TOT, OT, ET, NB = cfg.TT, cfg.TOT, cfg.OT, cfg.ET, cfg.NB
    QTOT = cfg.QTOT
    NL = cfg.n_layers
    NPAS = NB // 2048
    DI = cfg.DI
    NJ2 = (DI + 127) // 128

    def din(name, shape, dtype=F32):
        return nc.dram_tensor(name, list(shape), dtype, kind="ExternalInput").ap()

    def dout(name, shape, dtype=F32):
        return nc.dram_tensor(name, list(shape), dtype, kind="ExternalOutput").ap()

    def dscr(name, shape, dtype=F32):
        return nc.dram_tensor(name, list(shape), dtype, kind="Internal").ap()

    def sb(name, shape, dtype=F32):
        return nc.alloc_sbuf_tensor(name, list(shape), dtype)

    x_tot = din("x_tot", [TT, 128, C])
    x_full = din("x_full", [NB, C])
    w_qkv = din("w_qkv", [C, 3 * C])
    bqkv_c = din("bqkv_c", [128, 24])
    a_q_i = din("a_q", [HD, 16])
    b_q_i = din("b_q", [16, HD])
    a_v_i = din("a_v", [HD, 16])
    b_v_i = din("b_v", [16, HD])
    w_proj = din("w_proj", [C, C])
    bproj_r = din("bproj_r", [1, C])
    iWqkv = din("iWqkv", [NL, C, 3 * C])
    iqs = din("iqs", [NL, 128, 1])
    iks = din("iks", [NL, 128, 1])
    iWo = din("iWo", [NL, C, C])
    ig_c = din("ig_c", [NL, 128, 8])
    ib_c = din("ib_c", [NL, 128, 8])
    iW1 = din("iW1", [NL, C, 2 * DI])
    ib1a = din("ib1a", [NL, 128, 22])
    ib1g = din("ib1g", [NL, 128, 22])
    iW2 = din("iW2", [NL, DI, C])
    ib2 = din("ib2", [NL, 1, C])
    identi = din("identi", [128, 128], BF16)
    p2i = din("p2i", [128, 128], BF16)
    tabsi = din("tabsi", [4, 128, TOT], BF16)
    trii = din("trii", [128, 384])
    invi = din("invi", [1, TOT], BF16)
    zxi = din("zxi", [128, 1])

    out = dout("out", [OT, 128, C])

    zbuf = dscr("zbuf", [TT, 128, C])
    y1buf = dscr("y1buf", [OT, 128, C])
    kofull = dscr("kofull", [C, NB], BF16)
    vofull = dscr("vofull", [H, HD, NB], BF16)
    qofull = dscr("qofull", [H, HD, QTOT], BF16)
    nbrow = dscr("nbrow", [H, QTOT], BF16)
    aobuf = dscr("aobuf", [QTOT // 128, 128, C], BF16)
    qibuf = dscr("qibuf", [H, HD, TOT], BF16)
    vibuf = dscr("vibuf", [TT, 128, C], BF16)
    zffn = dscr("zffn", [NJ2, 128, TOT], BF16)
    kidbg = dscr("kidbg", [8, 128, TOT], BF16)
    aodbg = dscr("aodbg", [TT, 128, C], BF16)
    pdbg = dscr("pdbg", [2, 128, 512], BF16)
    qtdbg = dscr("qtdbg", [1, 128, 1024], BF16)
    s2dbg = dscr("s2dbg", [1, 128, 512])
    opdbg = dscr("opdbg", [2, 128, 512], BF16)
    scr = {"zbuf": zbuf, "kidbg": kidbg, "aodbg": aodbg, "pdbg": pdbg,
           "qtdbg": qtdbg, "s2dbg": s2dbg, "opdbg": opdbg, "y1buf": y1buf, "kofull": kofull, "vofull": vofull,
           "qofull": qofull, "nbrow": nbrow, "aobuf": aobuf, "qibuf": qibuf,
           "vibuf": vibuf, "zffn": zffn}
    dbg_outs = {}
    for n_ in cfg.debug:
        t_ = scr[n_]
        dt_ = F32 if n_ in ("zbuf", "y1buf", "s2dbg") else BF16
        dbg_outs[n_] = dout("dbg_" + n_, [int(np.prod(t_.shape[:-1])), t_.shape[-1]], dt_)

    identt = sb("identt", [128, 128], BF16)
    p2t = sb("p2t", [128, 128], BF16)
    trit = sb("trit", [128, 384])
    invt = sb("invt", [1, TOT], BF16)
    zxt = sb("zxt", [128, 1])
    onesrow = sb("onesrow", [1, 512], BF16)
    onescol = sb("onescol", [128, 1], BF16)
    oneblk = sb("oneblk", [128, 2], BF16)
    epst = sb("epst", [128, 1])
    aqt = sb("aqt", [128, 16], BF16)
    avt = sb("avt", [128, 16], BF16)
    bqlh = sb("bqlh", [16, 256], BF16)
    bvlh = sb("bvlh", [16, 256], BF16)
    bqkvt = sb("bqkvt", [128, 24])
    bprojt = sb("bprojt", [1, C], BF16)
    b2row = sb("b2row", [1, C], BF16)
    qscol = sb("qscol", [128, 1])
    kscol = sb("kscol", [128, 1])
    igcol = sb("igcol", [128, 8])
    ibbf = sb("ibbf", [128, 8], BF16)
    igbf = sb("igbf", [128, 8], BF16)
    obrlo = sb("obrlo", [1, 128], BF16)
    obrhi = sb("obrhi", [1, 128], BF16)
    b1ac = sb("b1ac", [128, 22])
    b1gc = sb("b1gc", [128, 22])
    kmaxr = sb("kmaxr", [1, 16])
    kallr = sb("kallr", [1, 1])
    ztile = [sb(f"ztile{i}", [128, C]) for i in range(2)]
    sqtile = sb("sqtile", [128, C])
    ztA = sqtile
    tmp1 = sb("tmp1", [128, 512])
    tmp2 = sb("tmp2", [128, 512])
    s2f = sb("s2f", [128, 384])
    rowsA = sb("rowsA", [1, 512])
    rowsB = sb("rowsB", [1, 512])
    rowsC = sb("rowsC", [1, 512])
    nbstg = sb("nbstg", [1, 512], BF16)
    nmx = sb("nmx", [128, 1])
    rc1 = sb("rc1", [128, 1])
    cA = sb("cA", [128, 1])
    cB = sb("cB", [128, 1])
    cC = sb("cC", [128, 1])
    cD = sb("cD", [128, 1])
    biasa = sb("biasa", [128, 1])
    biasg = sb("biasg", [128, 1])
    zbft = sb("zbft", [128, C], BF16)

    YTW = TOT + 1
    R1 = sb("R1", [128, max(9 * YTW, 8 * 2048, 8 * TOT)], BF16)
    R2sz = max(8 * TOT, 2 * NB + QTOT + 64 * 65, NJ2 * 1024, 8 * 1024)
    R2 = sb("R2", [128, R2sz], BF16)
    R4 = sb("R4", [128, 21000], BF16)

    def rsl(base, off, shape):
        n = int(np.prod(shape[1:]))
        ap = base.ap()[:, off:off + n]
        if len(shape) == 3:
            ap = ap.rearrange("p (a b) -> p a b", b=shape[2])
        return ap

    yT = [rsl(R1, i * YTW, [128, YTW]) for i in range(9)]
    hT = [rsl(R1, i * 2048, [128, 2048]) for i in range(8)]
    hTq = [rsl(R1, i * TOT, [128, TOT]) for i in range(8)]
    kres = [rsl(R2, i * TOT, [128, TOT]) for i in range(8)]
    kbar = rsl(R2, 0, [128, NB])
    qbar = rsl(R2, NB, [128, QTOT])
    vof = rsl(R2, NB + QTOT, [128, NB])
    vstage = rsl(R2, 2 * NB + QTOT, [128, 64, 65])
    w2res = [rsl(R2, i * 1024, [128, 1024]) for i in range(NJ2)]
    wpro = rsl(R1, 0, [128, 8, 1024])

    def mkcursor():
        state = {"c": 0}
        def take(shape, dtype_cols=None):
            n = int(np.prod(shape[1:]))
            off = state["c"]
            state["c"] += n
            assert state["c"] <= 21000, f"R4 overflow {state['c']}"
            return rsl(R4, off, shape)
        return take

    t = mkcursor()
    V1 = dict(wqs=[t([128, 8, 128]) for _ in range(2)], est=t([128, 512]),
              est2=t([128, 512]), t1sb=t([128, 1024]), sqb=t([128, 512]))
    t = mkcursor()
    V2 = dict(pexp=[t([128, 1024]) for _ in range(2)], obf=t([128, 512]),
              aostg=t([128, 4, 64]))
    t = mkcursor()
    V3 = dict(aot=[t([128, 1024]) for _ in range(2)], aofm=t([128, 1024]))
    t = mkcursor()
    VA = dict(wqs=[t([128, 8, 128]) for _ in range(2)],
              wvs=[t([128, 8, 512]) for _ in range(2)],
              est=t([128, 512]), sqb=t([128, 512]), rnbc=t([128, 512]),
              qsb=t([128, 512]), vstgf=t([128, 1024]),
              tabq=[t([128, TOT]) for _ in range(2)])
    t = mkcursor()
    VB = dict(qtile=t([128, 8, 128]), vstg3=[t([128, 16, 65]) for _ in range(3)],
              pbf=t([128, 384]), ptsb=t([128, 384]),
              aot=[t([128, 1024]) for _ in range(2)], aofm=t([128, 1024]))
    t = mkcursor()
    VF = dict(w1s=[t([128, 8, 256]) for _ in range(2)], sqb=t([128, 512]),
              nmrst=t([1, TOT]), rstdb=t([1, TOT]), browa=t([1, 256]),
              gg=t([128, 512]), zstg=t([128, 512]))
    t = mkcursor()
    VW = dict(zsrm=[t([128, 256]) for _ in range(2)])

    pA0 = nc.alloc_psum_tensor("pA0", [128, 512], F32)
    pA1 = nc.alloc_psum_tensor("pA1", [128, 512], F32)
    pB = nc.alloc_psum_tensor("pB", [128, 1024], F32)
    pO = nc.alloc_psum_tensor("pO", [128, 512], F32)
    pL = nc.alloc_psum_tensor("pL", [128, 512], F32)
    pR = nc.alloc_psum_tensor("pR", [128, 512], F32)
    pT = nc.alloc_psum_tensor("pT", [128, 1024], BF16)

    prog = P(nc)
    E = prog.emit
    W_ = prog.wait
    MM = lambda fn: prog.emit("tensor", fn, "MM")
    VE = lambda fn: prog.emit("vector", fn, "VE")
    AC = lambda fn: prog.emit("scalar", fn, "AC")

    for dst, src in ((identt, identi), (p2t, p2i), (trit, trii), (zxt, zxi),
                     (bqkvt, bqkv_c)):
        prog.dma("sync", dst[:, :], src, "Dc")
    prog.dma("sync", invt[:, :], invi, "Dc")
    for dst, src in ((bprojt[:, :], bproj_r),
                     (aqt[0:64, :], a_q_i), (aqt[64:128, :], a_q_i),
                     (avt[0:64, :], a_v_i), (avt[64:128, :], a_v_i),
                     (bqlh[:, 0:64], b_q_i), (bqlh[:, 192:256], b_q_i),
                     (bvlh[:, 0:64], b_v_i), (bvlh[:, 192:256], b_v_i)):
        prog.dma("gpsimd", dst, src, "Dc2")
    vld, vwl = prog.val("Dc"), prog.val("Dc2")
    W_("vector", "Dc", vld)
    W_("vector", "Dc2", vwl)
    for fn in (lambda v: v.memset(onesrow[:, :], 1.0),
               lambda v: v.memset(onescol[:, :], 1.0),
               lambda v: v.memset(epst[:, :], 1e-5),
               lambda v: v.memset(oneblk[:, :], 0.0),
               lambda v: v.memset(kmaxr[:, :], 0.0),
               lambda v: v.memset(bqlh[:, 64:192], 0.0),
               lambda v: v.memset(bvlh[:, 64:192], 0.0)):
        E("vector", fn)
    prog.drain("vector")
    E("vector", lambda v: v.memset(oneblk[0:64, 0:1], 1.0))
    E("vector", lambda v: v.memset(oneblk[64:128, 1:2], 1.0))
    E("vector", lambda v: v.memset(obrlo[:, :], 0.0))
    E("vector", lambda v: v.memset(obrhi[:, :], 0.0))
    prog.drain("vector")
    E("vector", lambda v: v.memset(obrlo[0:1, 0:64], 1.0))
    E("vector", lambda v: v.memset(obrhi[0:1, 64:128], 1.0))
    E("vector", lambda v: v.memset(vstage[:, :, :], 1.0))
    E("vector", lambda v: v.memset(V1["est"], 0.0))
    for i in range(9):
        E("vector", lambda v, i=i: v.memset(yT[i][:, 0:1], 0.0))
    prog.drain("vector")
    vz = VE(lambda v: v.memset(tmp1[:, :], 0.0))
    W_("scalar", "Dc2", vwl)
    W_("scalar", "VE", vz)
    E("scalar", lambda s: s.mul(bqlh[:, :], bqlh[:, :], 0.25))
    vsc = AC(lambda s: s.mul(bvlh[:, :], bvlh[:, :], 2.0))
    W_("sync", "VE", vz)
    if QTOT > TOT:
        npad = QTOT - TOT
        for h in range(H):
            prog.dma("sync", qofull[h, :, TOT:QTOT], V1["est"][0:HD, 0:npad], "Sest")
            prog.dma("sync", nbrow[h:h + 1, TOT:QTOT], V1["est"][0:1, 0:npad], "Sest")
    if cfg.debug:
        vzz = VE(lambda v: v.memset(zbft[:, :], 0.0))
        vzz2 = VE(lambda v: v.memset(ztile[0][:, :], 0.0))
        W_("sync", "VE", vzz2)
        for n_ in cfg.debug:
            t_ = scr[n_]
            zsrc = ztile[0] if t_.dtype == F32 else zbft
            rows = int(np.prod(t_.shape[:-1]))
            cols = t_.shape[-1]
            flat = t_.rearrange("a p c -> (a p) c") if len(t_.shape) == 3 else t_
            for r0 in range(0, rows, 128):
                nr = min(128, rows - r0)
                for c0 in range(0, cols, 1024):
                    ncol = min(1024, cols - c0)
                    prog.dma("sync", flat[r0:r0 + nr, c0:c0 + ncol],
                             zsrc[0:nr, 0:ncol], "Sdbg")
        W_("sync", "Sdbg", prog.val("Sdbg"))
    prog.barrier()

    bank_free = {}

    def bfree(pp, sem, cnt):
        bank_free[id(pp)] = (sem, cnt)

    def bwait(pp):
        if id(pp) in bank_free:
            s, c = bank_free.pop(id(pp))
            W_("tensor", s, c)

    stg_free = {}

    def stg_wait(eng, key):
        if key in stg_free:
            s, c = stg_free.pop(key)
            W_(eng, s, c)

    def stg_mark(key, sem="ST"):
        stg_free[key] = (sem, prog.val(sem))


    def ln_tile(src, dst_bf, zx=False):
        VE(lambda v: v.reduce_sum(cA[:, :], src, AX.X, negate=True))
        va = AC(lambda s: s.activation(sqtile[:, :], src, AF.Square))
        W_("vector", "AC", va)
        VE(lambda v: v.reduce_sum(cB[:, :], sqtile[:, :], AX.X))
        prog.drain("vector")
        E("vector", lambda v: v.tensor_scalar(cA[:, :], cA[:, :], 1.0 / C, None, ALU.mult))
        E("vector", lambda v: v.tensor_scalar(cB[:, :], cB[:, :], 1.0 / C, None, ALU.mult))
        prog.drain("vector")
        E("vector", lambda v: v.tensor_mul(cC[:, :], cA[:, :], cA[:, :]))
        prog.drain("vector")
        vv = VE(lambda v: v.tensor_sub(cB[:, :], cB[:, :], cC[:, :]))
        W_("scalar", "VE", vv)
        E("scalar", lambda s: s.activation(cB[:, :], cB[:, :], AF.Ln, bias=epst[:, :]))
        prog.drain("scalar")
        va = AC(lambda s: s.activation(cB[:, :], cB[:, :], AF.Exp, scale=-0.5))
        W_("vector", "AC", va)
        if zx:
            E("vector", lambda v: v.tensor_scalar(src, src, zxt[:, :], None, ALU.mult))
            prog.drain("vector")
        stg_wait("vector", "zbft")
        return VE(lambda v: v.tensor_scalar(dst_bf, src, cA[:, :], cB[:, :],
                                            ALU.add, ALU.mult))

    pt_ev = [0, 0, 0, 0, 0, 0, 0, 0]
    tr_hist = []

    TR_DIST = 1

    def tr_gate():
        if len(tr_hist) >= TR_DIST:
            W_("tensor", "VE", tr_hist[-TR_DIST])

    def transpose1(src128, dst128, sl):
        if pt_ev[sl]:
            W_("tensor", "VE", pt_ev[sl])
        tr_gate()
        vm = MM(lambda t: t.transpose(pT[:, sl * 128:(sl + 1) * 128], src128, identt[:, :]))
        W_("vector", "MM", vm)
        ev = VE(lambda v: v.tensor_copy(dst128, pT[:, sl * 128:(sl + 1) * 128]))
        pt_ev[sl] = ev
        tr_hist.append(ev)
        return ev

    def transpose8(src_bf, dst8):
        if cfg.ln_mode == "tr_v1":
            for ch in range(8):
                transpose1(src_bf[:, ch * 128:(ch + 1) * 128], dst8[ch], ch)
            return prog.val("VE")
        if cfg.ln_mode == "tr_v2":
            for ch in range(8):
                sl = ch % 4
                if pt_ev[sl]:
                    W_("tensor", "VE", pt_ev[sl])
                vm = MM(lambda t, ch=ch, sl=sl: t.transpose(
                    pT[:, sl * 128:(sl + 1) * 128], src_bf[:, ch * 128:(ch + 1) * 128], identt[:, :]))
                W_("vector", "MM", vm)
                ev = VE(lambda v, ch=ch, dst=dst8[ch]: v.tensor_copy(dst, src_bf[:, ch * 128:(ch + 1) * 128]))
                pt_ev[sl] = ev
            return prog.val("VE")
        if cfg.ln_mode == "tr_v3":
            for ch in range(8):
                sl = ch % 4
                if pt_ev[sl]:
                    W_("tensor", "VE", pt_ev[sl])
                vm = MM(lambda t: t.nop())
                W_("vector", "MM", vm)
                ev = VE(lambda v, sl=sl, dst=dst8[ch]: v.tensor_copy(dst, pT[:, sl * 128:(sl + 1) * 128]))
                pt_ev[sl] = ev
            return prog.val("VE")
        if cfg.ln_mode == "slots47":
            for ch in range(cfg.tr_n):
                transpose1(src_bf[:, ch * 128:(ch + 1) * 128], dst8[ch], 4 + ch % 4)
            return prog.val("VE")
        if cfg.ln_mode == "dst0":
            for ch in range(cfg.tr_n):
                transpose1(src_bf[:, ch * 128:(ch + 1) * 128], dst8[0], ch % 4)
            return prog.val("VE")
        for ch in range(cfg.tr_n):
            transpose1(src_bf[:, ch * 128:(ch + 1) * 128], dst8[ch], ch % 4)
        return prog.val("VE")

    def tile_chunks(t0, t1, maxn=4):
        outl = []
        tc = t0
        while tc < t1:
            n = min(maxn, t1 - tc)
            outl.append((tc, n))
            tc += n
        return outl

    zt_free = [0, 0]

    def load_ln_transpose(src_ap, idx, dst8, zx=False):
        zi = ztile[idx % 2]
        W_("sync", "VE", zt_free[idx % 2])
        stg_wait("sync", f"zt{idx % 2}")
        vput = prog.dma("sync", zi[:, :], src_ap, f"Dzt{idx % 2}")
        if cfg.ln_mode == "dma":
            zt_free[idx % 2] = 0
            return
        W_("vector", f"Dzt{idx % 2}", vput)
        W_("scalar", f"Dzt{idx % 2}", vput)
        ln_tile(zi[:, :], zbft[:, :], zx=zx)
        zt_free[idx % 2] = prog.val("VE")
        if cfg.ln_mode == "ln":
            return
        W_("tensor", "VE", zt_free[idx % 2])
        transpose8(zbft[:, :], dst8)
        stg_free["zbft"] = ("MM", prog.val("MM"))

    def load_cast_transpose(src_ap, idx, dst8, zx=False):
        zi = ztile[idx % 2]
        W_("sync", "VE", zt_free[idx % 2])
        stg_wait("sync", f"zt{idx % 2}")
        vput = prog.dma("sync", zi[:, :], src_ap, f"Dzt{idx % 2}")
        W_("vector", f"Dzt{idx % 2}", vput)
        stg_wait("vector", "zbft")
        if zx:
            VE(lambda v: v.tensor_scalar(zbft[:, :], zi[:, :], zxt[:, :], None, ALU.mult))
        else:
            VE(lambda v: v.tensor_copy(zbft[:, :], zi[:, :]))
        zt_free[idx % 2] = prog.val("VE")
        prog.drain("vector")
        W_("tensor", "VE", zt_free[idx % 2])
        transpose8(zbft[:, :], dst8)
        stg_free["zbft"] = ("MM", prog.val("MM"))

    if cfg.phase >= 1:
        for p in range(NPAS if cfg.o1_p is None else cfg.o1_p):
            for tt in range(cfg.o1_tiles):
                (load_cast_transpose if cfg.ln_mode in ("cast","slots47","dst0") else load_ln_transpose)(
                    x_full[(p * 16 + tt) * 128:(p * 16 + tt + 1) * 128, :],
                                  tt, [hT[ch][:, tt * 128:(tt + 1) * 128] for ch in range(8)])
            vhT = prog.val("VE")
            for kind in (() if cfg.o1_ln_only else tuple(cfg.o1_kinds)):
                cbase = 8 if kind == "k" else 16
                wbase = C if kind == "k" else 2 * C
                for cc in range(cfg.o1_cc):
                    wb = V1["wqs"][cc % 2]
                    stg_wait("gpsimd", f"wq{cc % 2}")
                    vw = prog.dma("gpsimd", wb,
                                  w_qkv[:, wbase + cc * 128: wbase + (cc + 1) * 128]
                                  .rearrange("(k p) n -> p k n", p=128), f"Dw{cc % 2}")
                    W_("tensor", f"Dw{cc % 2}", vw)
                    W_("tensor", "VE", vhT)
                    for tc in range(4):
                        pp = pA0 if tc % 2 == 0 else pA1
                        bwait(pp)
                        for kc in range(8):
                            vm = MM(lambda t, wb=wb, kc=kc, tc=tc, pp=pp: t.matmul(
                                pp[:, :], wb[:, kc, :], hT[kc][:, tc * 512:(tc + 1) * 512],
                                start=(kc == 0), stop=(kc == 7)))
                        W_("vector", "MM", vm)
                        stg_wait("vector", "o1est")
                        vv = VE(lambda v, pp=pp, cb=cbase + cc: v.tensor_scalar(
                            V1["est"], pp[:, :], bqkvt[:, cb:cb + 1], None, ALU.add))
                        bfree(pp, "VE", vv)
                        tcs = p * 2048 + tc * 512
                        if kind == "k":
                            W_("sync", "VE", vv)
                            prog.dma("sync", kofull[cc * 128:(cc + 1) * 128, tcs:tcs + 512],
                                     V1["est"], "Sest")
                            stg_free["o1est"] = ("Sest", prog.val("Sest"))
                            W_("scalar", "VE", vv)
                            va = AC(lambda s: s.activation(V1["sqb"], V1["est"], AF.Square))
                            bwait(pR)
                            bwait(pL)
                            W_("tensor", "AC", va)
                            MM(lambda t: t.matmul(pR[0:1, 0:512], oneblk[:, 0:1], V1["sqb"],
                                                  start=True, stop=True))
                            m2 = MM(lambda t: t.matmul(pL[0:1, 0:512], oneblk[:, 1:2],
                                                       V1["sqb"], start=True, stop=True))
                            W_("vector", "MM", m2)
                            E("vector", lambda v: v.reduce_max(cC[0:1, :], pR[0:1, 0:512], AX.X))
                            E("vector", lambda v: v.reduce_max(cD[0:1, :], pL[0:1, 0:512], AX.X))
                            prog.drain("vector")
                            E("vector", lambda v, cc=cc: v.tensor_max(
                                kmaxr[0:1, 2 * cc:2 * cc + 1],
                                kmaxr[0:1, 2 * cc:2 * cc + 1], cC[0:1, :]))
                            vv2 = VE(lambda v, cc=cc: v.tensor_max(
                                kmaxr[0:1, 2 * cc + 1:2 * cc + 2],
                                kmaxr[0:1, 2 * cc + 1:2 * cc + 2], cD[0:1, :]))
                            prog.drain("vector")
                            bfree(pR, "VE", vv2)
                            bfree(pL, "VE", vv2)
                        else:
                            bwait(pL)
                            bwait(pR)
                            W_("tensor", "VE", vv)
                            MM(lambda t: t.matmul(pL[0:16, 0:512], avt[0:64, :],
                                                  V1["est"][0:64, :], start=True, stop=True))
                            m2 = MM(lambda t: t.matmul(pR[0:16, 0:512], avt[64:128, :],
                                                       V1["est"][64:128, :], start=True, stop=True))
                            W_("vector", "MM", m2)
                            E("vector", lambda v: v.tensor_copy(V1["t1sb"][0:16, 0:512],
                                                                pL[0:16, 0:512]))
                            vv2 = VE(lambda v: v.tensor_copy(V1["t1sb"][0:16, 512:1024],
                                                             pR[0:16, 0:512]))
                            bfree(pL, "VE", vv2)
                            bfree(pR, "VE", vv2)
                            bwait(pO)
                            W_("tensor", "VE", vv2)
                            MM(lambda t: t.matmul(pO[:, :], bvlh[:, 0:128],
                                                  V1["t1sb"][0:16, 0:512], start=True, stop=False))
                            m4 = MM(lambda t: t.matmul(pO[:, :], bvlh[:, 128:256],
                                                       V1["t1sb"][0:16, 512:1024],
                                                       start=False, stop=True))
                            W_("vector", "MM", m4)
                            stg_wait("vector", "o1est2")
                            vv3 = VE(lambda v: v.tensor_add(V1["est2"], V1["est"], pO[:, :]))
                            stg_free["o1est"] = ("VE", vv3)
                            bfree(pO, "VE", vv3)
                            W_("sync", "VE", vv3)
                            for hh in range(2):
                                prog.dma("sync", vofull[2 * cc + hh, :, tcs:tcs + 512],
                                         V1["est2"][64 * hh:64 * hh + 64, :], "Sest2")
                            stg_free["o1est2"] = ("Sest2", prog.val("Sest2"))
                        stg_free[f"wq{cc % 2}"] = ("MM", prog.val("MM"))
            prog.barrier()

        E("vector", lambda v: v.tensor_scalar(kmaxr[0:1, :], kmaxr[0:1, :], 1e-24, None, ALU.max))
        prog.drain("vector")
        vv = VE(lambda v: v.reduce_max(kallr[0:1, :], kmaxr[0:1, :], AX.X))
        W_("scalar", "VE", vv)
        E("scalar", lambda s: s.activation(kallr[0:1, :], kallr[0:1, :], AF.Ln))
        prog.drain("scalar")
        vka = AC(lambda s: s.activation(kallr[0:1, :], kallr[0:1, :], AF.Exp, scale=0.5))
        W_("vector", "AC", vka)

    if cfg.phase >= 2:
        for tt in range(TT):
            load_ln_transpose(x_tot[tt], tt,
                              [hTq[ch][:, tt * 128:(tt + 1) * 128] for ch in range(8)])
        vhT = prog.val("VE")
        for cc in range(8):
            wb = V1["wqs"][cc % 2]
            stg_wait("gpsimd", f"wq{cc % 2}")
            vw = prog.dma("gpsimd", wb,
                          w_qkv[:, cc * 128:(cc + 1) * 128].rearrange("(k p) n -> p k n", p=128),
                          f"Dw{cc % 2}")
            W_("tensor", f"Dw{cc % 2}", vw)
            W_("tensor", "VE", vhT)
            for gi, (g0, gn) in enumerate(tile_chunks(0, TT, 4)):
                n = gn * 128
                c0 = g0 * 128
                pp = pA0 if gi % 2 == 0 else pA1
                bwait(pp)
                for kc in range(8):
                    vm = MM(lambda t, wb=wb, kc=kc, pp=pp, c0=c0, n=n: t.matmul(
                        pp[:, 0:n], wb[:, kc, :], hTq[kc][:, c0:c0 + n],
                        start=(kc == 0), stop=(kc == 7)))
                W_("vector", "MM", vm)
                stg_wait("vector", "o1est")
                vv = VE(lambda v, pp=pp, cc=cc, n=n: v.tensor_scalar(
                    V1["est"][:, 0:n], pp[:, 0:n], bqkvt[:, cc:cc + 1], None, ALU.add))
                bfree(pp, "VE", vv)
                bwait(pL)
                bwait(pR)
                W_("tensor", "VE", vv)
                MM(lambda t, n=n: t.matmul(pL[0:16, 0:n], aqt[0:64, :], V1["est"][0:64, 0:n],
                                           start=True, stop=True))
                m2 = MM(lambda t, n=n: t.matmul(pR[0:16, 0:n], aqt[64:128, :],
                                                V1["est"][64:128, 0:n], start=True, stop=True))
                W_("vector", "MM", m2)
                E("vector", lambda v, n=n: v.tensor_copy(V1["t1sb"][0:16, 0:n], pL[0:16, 0:n]))
                vv2 = VE(lambda v, n=n: v.tensor_copy(V1["t1sb"][0:16, 512:512 + n],
                                                      pR[0:16, 0:n]))
                bfree(pL, "VE", vv2)
                bfree(pR, "VE", vv2)
                bwait(pO)
                W_("tensor", "VE", vv2)
                MM(lambda t, n=n: t.matmul(pO[:, 0:n], bqlh[:, 0:128], V1["t1sb"][0:16, 0:n],
                                           start=True, stop=False))
                m4 = MM(lambda t, n=n: t.matmul(pO[:, 0:n], bqlh[:, 128:256],
                                                V1["t1sb"][0:16, 512:512 + n],
                                                start=False, stop=True))
                W_("vector", "MM", m4)
                stg_wait("vector", "o1est2")
                vv3 = VE(lambda v, n=n: v.scalar_tensor_tensor(
                    V1["est2"][:, 0:n], V1["est"][:, 0:n], 0.125, pO[:, 0:n],
                    ALU.mult, ALU.add))
                stg_free["o1est"] = ("VE", vv3)
                bfree(pO, "VE", vv3)
                W_("sync", "VE", vv3)
                for hh in range(2):
                    prog.dma("sync", qofull[2 * cc + hh, :, c0:c0 + n],
                             V1["est2"][64 * hh:64 * hh + 64, 0:n], "Sest2")
                stg_free["o1est2"] = ("Sest2", prog.val("Sest2"))
                W_("scalar", "VE", vv3)
                va = AC(lambda s, n=n: s.activation(V1["sqb"][:, 0:n], V1["est2"][:, 0:n],
                                                    AF.Square))
                bwait(pL)
                bwait(pR)
                W_("tensor", "AC", va)
                MM(lambda t, n=n: t.matmul(pR[0:1, 0:n], oneblk[:, 0:1], V1["sqb"][:, 0:n],
                                           start=True, stop=True))
                m6 = MM(lambda t, n=n: t.matmul(pL[0:1, 0:n], oneblk[:, 1:2], V1["sqb"][:, 0:n],
                                                start=True, stop=True))
                W_("vector", "MM", m6)
                lastv = 0
                for hh, psrc in ((0, pR), (1, pL)):
                    vv4 = VE(lambda v, psrc=psrc, n=n: v.tensor_scalar(
                        rowsA[0:1, 0:n], psrc[0:1, 0:n], 1e-24, None, ALU.max))
                    W_("scalar", "VE", vv4)
                    E("scalar", lambda s, n=n: s.activation(rowsA[0:1, 0:n], rowsA[0:1, 0:n],
                                                            AF.Ln))
                    prog.drain("scalar")
                    va = AC(lambda s, n=n: s.activation(rowsA[0:1, 0:n], rowsA[0:1, 0:n],
                                                        AF.Exp, scale=0.5))
                    W_("vector", "AC", va)
                    E("vector", lambda v, n=n: v.tensor_scalar(
                        rowsB[0:1, 0:n], rowsA[0:1, 0:n], kallr[0:1, 0:1], -1.0,
                        ALU.mult, ALU.mult))
                    prog.drain("vector")
                    stg_wait("vector", "o1nbs")
                    vv5 = VE(lambda v, n=n: v.tensor_copy(nbstg[0:1, 0:n], rowsB[0:1, 0:n]))
                    W_("sync", "VE", vv5)
                    prog.dma("sync", nbrow[2 * cc + hh:2 * cc + hh + 1, c0:c0 + n],
                             nbstg[0:1, 0:n], "Snb")
                    stg_free["o1nbs"] = ("Snb", prog.val("Snb"))
                    lastv = vv5
                bfree(pR, "VE", lastv)
                bfree(pL, "VE", lastv)
            stg_free[f"wq{cc % 2}"] = ("MM", prog.val("MM"))
        prog.barrier()

    if cfg.phase >= 3:
        NKT = NB // 128
        NQC = QTOT // 512
        sT_banks = [(pB[:, 0:512], pB[:, 512:1024]), (pL[:, :], pR[:, :])]
        E("vector", lambda v: v.memset(V2["obf"][:, :], 0.0))
        prog.drain("vector")
        head_mm_done = 0
        for h in range(H):
            W_("sync", "MM", head_mm_done)
            vk = prog.dma("sync", kbar[0:64, :], kofull[h * 64:(h + 1) * 64, :], "Dk2")
            if h == 0:
                W_("vector", "Dk2", vk)
                vone = VE(lambda v: v.memset(kbar[64:65, :], 1.0))
                W_("tensor", "VE", vone)
            prog.dma("sync", qbar[0:64, :], qofull[h], "Dq2")
            prog.dma("sync", qbar[64:65, :], nbrow[h:h + 1, :], "Dq2")
            vvf = prog.dma("sync", vof[0:64, :], vofull[h], "Dv2")
            W_("tensor", "Dk2", prog.val("Dk2"))
            W_("tensor", "Dq2", prog.val("Dq2"))
            W_("tensor", "Dv2", prog.val("Dv2"))
            for ktq in range(NKT // 4):
                for j in range(4):
                    sl = j
                    if pt_ev[sl]:
                        W_("tensor", "VE", pt_ev[sl])
                    tr_gate()
                    vm = MM(lambda t, ktq=ktq, j=j, sl=sl: t.transpose(
                        pT[:, sl * 128:sl * 128 + 64],
                        vof[0:64, (ktq * 4 + j) * 128:(ktq * 4 + j + 1) * 128], identt[0:64, 0:64]))
                    W_("vector", "MM", vm)
                    ev = VE(lambda v, ktq=ktq, j=j, sl=sl: v.tensor_copy(
                        vstage[:, ktq * 4 + j, 0:64], pT[:, sl * 128:sl * 128 + 64]))
                    pt_ev[sl] = ev
                    tr_hist.append(ev)
            vstage_done = prog.val("VE")
            W_("tensor", "VE", vstage_done)
            for qc in range(NQC):
                q0 = qc * 512
                bwait(pO)
                exp_done = [0, 0]
                pv_done = 0
                for kk in range(NKT // 2):
                    ping = kk % 2
                    b0, b1 = sT_banks[ping]
                    if exp_done[ping]:
                        W_("tensor", "AC", exp_done[ping])
                    k0 = kk * 2
                    MM(lambda t, b0=b0, k0=k0, q0=q0: t.matmul(
                        b0, kbar[0:65, k0 * 128:(k0 + 1) * 128], qbar[0:65, q0:q0 + 512],
                        start=True, stop=True))
                    vm = MM(lambda t, b1=b1, k0=k0, q0=q0: t.matmul(
                        b1, kbar[0:65, (k0 + 1) * 128:(k0 + 2) * 128], qbar[0:65, q0:q0 + 512],
                        start=True, stop=True))
                    W_("scalar", "MM", vm)
                    if pv_done:
                        W_("scalar", "MM", pv_done)
                    if ping == 0:
                        va = AC(lambda s: s.activation(V2["pexp"][0], pB[:, :], AF.Exp))
                    else:
                        AC(lambda s: s.activation(V2["pexp"][1][:, 0:512], pL[:, :], AF.Exp))
                        va = AC(lambda s: s.activation(V2["pexp"][1][:, 512:1024], pR[:, :],
                                                       AF.Exp))
                    exp_done[ping] = va
                    W_("tensor", "AC", va)
                    MM(lambda t, k0=k0, ping=ping: t.matmul(
                        pO[0:65, :], vstage[:, k0, :], V2["pexp"][ping][:, 0:512],
                        start=(k0 == 0), stop=False, skip_group_check=True))
                    pv_done = MM(lambda t, k0=k0, ping=ping: t.matmul(
                        pO[0:65, :], vstage[:, k0 + 1, :], V2["pexp"][ping][:, 512:1024],
                        start=False, stop=(k0 + 1 == NKT - 1), skip_group_check=True))
                head_mm_done = pv_done
                W_("vector", "MM", pv_done)
                vv = VE(lambda v: v.tensor_copy(V2["obf"][0:65, :], pO[0:65, :]))
                bfree(pO, "VE", vv)
                stg_wait("vector", "aostg")
                for qq in range(4):
                    sl = 4 + qq % 2
                    if pt_ev[sl]:
                        W_("tensor", "VE", pt_ev[sl])
                    tr_gate()
                    W_("tensor", "VE", vv)
                    vm = MM(lambda t, qq=qq, sl=sl: t.transpose(
                        pT[:, sl * 128:(sl + 1) * 128], V2["obf"][:, qq * 128:(qq + 1) * 128],
                        identt[:, :]))
                    W_("vector", "MM", vm)
                    prog.drain("vector")
                    E("vector", lambda v, sl=sl: v.reciprocal(
                        rc1[:, :], pT[:, sl * 128 + 64:sl * 128 + 65]))
                    prog.drain("vector")
                    ev = VE(lambda v, qq=qq, sl=sl: v.tensor_scalar(
                        V2["aostg"][:, qq, :], pT[:, sl * 128:sl * 128 + 64], rc1[:, :],
                        None, ALU.mult))
                    pt_ev[sl] = ev
                    tr_hist.append(ev)
                W_("sync", "VE", prog.val("VE"))
                prog.dma("sync",
                         aobuf[qc * 4:qc * 4 + 4, :, h * 64:(h + 1) * 64]
                         .rearrange("a p c -> p a c"),
                         V2["aostg"], "Sao")
                stg_mark("aostg", "Sao")
        prog.barrier()

    if cfg.phase >= 4:
        vw = prog.dma("gpsimd", wpro, w_proj.rearrange("(k p) n -> p k n", p=128), "Dwp")
        W_("tensor", "Dwp", vw)

        for idx, j in enumerate(range(TT)):
            ab = V3["aot"][idx % 2]
            stg_wait("sync", f"aot{idx % 2}")
            vld2 = prog.dma("sync", ab, aobuf[j], f"Dao{idx % 2}")
            W_("tensor", f"Dao{idx % 2}", vld2)
            for ch in range(8):
                transpose1(ab[:, ch * 128:(ch + 1) * 128],
                           V3["aofm"][:, ch * 128:(ch + 1) * 128], ch % 4)
            stg_free[f"aot{idx % 2}"] = ("MM", prog.val("MM"))
            vtr = prog.val("VE")
            W_("tensor", "VE", vtr)
            bwait(pB)
            for oc in range(2):
                for ch in range(8):
                    MM(lambda t, ch=ch, oc=oc: t.matmul(
                        pB[:, oc * 512:(oc + 1) * 512],
                        V3["aofm"][:, ch * 128:(ch + 1) * 128],
                        wpro[:, ch, oc * 512:(oc + 1) * 512],
                        start=(ch == 0), stop=False))
                vm = MM(lambda t, oc=oc: t.matmul(
                    pB[:, oc * 512:(oc + 1) * 512], onesrow[0:1, 0:128],
                    bprojt[0:1, oc * 512:(oc + 1) * 512], start=False, stop=True))
            zi = ztile[idx % 2]
            stg_wait("sync", f"zt{idx % 2}")
            vput = prog.dma("sync", zi[:, :], x_tot[j], f"Dzt{idx % 2}")
            W_("vector", f"Dzt{idx % 2}", vput)
            W_("vector", "MM", vm)
            E("vector", lambda v, zi=zi: v.tensor_add(zi[:, 0:512], zi[:, 0:512], pB[:, 0:512]))
            vv = VE(lambda v, zi=zi: v.tensor_add(zi[:, 512:1024], zi[:, 512:1024],
                                                  pB[:, 512:1024]))
            bfree(pB, "VE", vv)
            prog.drain("vector")
            if j >= ET:
                W_("sync", "VE", vv)
                prog.dma("sync", y1buf[j - ET], zi[:, :], f"Szt{idx % 2}")
            W_("scalar", "VE", vv)
            va = AC(lambda s, zi=zi: s.activation(sqtile[:, :], zi[:, :], AF.Square))
            VE(lambda v, zi=zi: v.reduce_sum(cA[:, :], zi[:, :], AX.X, negate=True))
            W_("vector", "AC", va)
            VE(lambda v: v.reduce_sum(cB[:, :], sqtile[:, :], AX.X))
            prog.drain("vector")
            E("vector", lambda v: v.tensor_scalar(cA[:, :], cA[:, :], 1.0 / C, None, ALU.mult))
            E("vector", lambda v: v.tensor_scalar(cB[:, :], cB[:, :], 1.0 / C, None, ALU.mult))
            prog.drain("vector")
            E("vector", lambda v: v.tensor_mul(cC[:, :], cA[:, :], cA[:, :]))
            prog.drain("vector")
            vv = VE(lambda v: v.tensor_sub(cB[:, :], cB[:, :], cC[:, :]))
            W_("scalar", "VE", vv)
            E("scalar", lambda s: s.activation(cB[:, :], cB[:, :], AF.Ln, bias=epst[:, :]))
            prog.drain("scalar")
            va = AC(lambda s: s.activation(cB[:, :], cB[:, :], AF.Exp, scale=-0.5))
            W_("vector", "AC", va)
            if j >= ET:
                W_("vector", f"Szt{idx % 2}", prog.val(f"Szt{idx % 2}"))
            vv = VE(lambda v, zi=zi: v.tensor_scalar(zi[:, :], zi[:, :], cA[:, :], cB[:, :],
                                                     ALU.add, ALU.mult))
            zt_free[idx % 2] = vv
            W_("sync", "VE", vv)
            vst = prog.dma("sync", zbuf[j], zi[:, :], f"Szt{idx % 2}")
            zt_free[idx % 2] = None
            stg_free[f"zt{idx % 2}"] = (f"Szt{idx % 2}", prog.val(f"Szt{idx % 2}"))
        prog.barrier()
    zt_free = [0, 0]
    stg_free.clear()

    def build_yT(t_lo, zx_ext=True):
        for idx, j in enumerate(range(t_lo, TT)):
            load_cast_transpose(
                zbuf[j], idx,
                [yT[slot[ch]][:, 1 + j * 128:1 + (j + 1) * 128] for ch in range(8)],
                zx=(zx_ext and j < ET))
        return prog.val("VE")

    def rot_norm_store(kind, c0, n, pp, dst_sb=None, dst_dram=None):
        scol = qscol if kind == "q" else kscol
        tb = VA["tabq"]
        stg_wait("scalar", "Asqb")
        va = AC(lambda s: s.activation(VA["sqb"][:, 0:n], pp[:, 0:n], AF.Square))
        bwait(pL)
        bwait(pR)
        W_("tensor", "AC", va)
        MM(lambda t: t.matmul(pR[0:1, 0:n], oneblk[:, 0:1], VA["sqb"][:, 0:n],
                              start=True, stop=True))
        m2 = MM(lambda t: t.matmul(pL[0:1, 0:n], oneblk[:, 1:2], VA["sqb"][:, 0:n],
                                   start=True, stop=True))
        stg_free["Asqb"] = ("MM", m2)
        W_("vector", "MM", m2)
        m4 = None
        for hh, psrc, obr in ((0, pR, obrlo), (1, pL, obrhi)):
            vv = VE(lambda v, psrc=psrc: v.tensor_scalar(rowsA[0:1, 0:n], psrc[0:1, 0:n],
                                                         1e-24, None, ALU.max))
            W_("scalar", "VE", vv)
            E("scalar", lambda s: s.activation(rowsA[0:1, 0:n], rowsA[0:1, 0:n], AF.Ln))
            prog.drain("scalar")
            va = AC(lambda s: s.activation(rowsA[0:1, 0:n], rowsA[0:1, 0:n],
                                           AF.Exp, scale=-0.5))
            W_("vector", "AC", va)
            if hh == 1 and m4 is not None:
                W_("vector", "MM", m4)
            vv = VE(lambda v: v.tensor_copy(nbstg[0:1, 0:n], rowsA[0:1, 0:n]))
            prog.drain("vector")
            if hh == 0:
                bwait(pO)
            W_("tensor", "VE", vv)
            m4 = MM(lambda t, obr=obr, hh=hh: t.matmul(
                pO[:, 0:n], obr[0:1, :], nbstg[0:1, 0:n],
                start=(hh == 0), stop=(hh == 1)))
        vnow = prog.val("VE")
        bfree(pR, "VE", vnow)
        bfree(pL, "VE", vnow)
        W_("vector", "MM", m4)
        vv = VE(lambda v: v.tensor_copy(VA["rnbc"][:, 0:n], pO[:, 0:n]))
        bfree(pO, "VE", vv)
        prog.drain("vector")
        stg_wait("vector", "Aqsb")
        vv = VE(lambda v: v.scalar_tensor_tensor(VA["qsb"][:, 0:n], pp[:, 0:n],
                                                 scol[:, :], VA["rnbc"][:, 0:n],
                                                 ALU.mult, ALU.mult))
        bfree(pp, "VE", vv)
        bwait(pB)
        W_("tensor", "VE", vv)
        vm = MM(lambda t: t.matmul(pB[:, 0:n], p2t[:, :], VA["qsb"][:, 0:n],
                                   start=True, stop=True))
        stg_free["Aqsb"] = ("MM", vm)
        W_("vector", "MM", vm)
        E("vector", lambda v: v.tensor_mul(tmp1[:, 0:n], VA["qsb"][:, 0:n],
                                           tb[0][:, c0:c0 + n]))
        vv = VE(lambda v: v.tensor_mul(tmp2[:, 0:n], pB[:, 0:n], tb[1][:, c0:c0 + n]))
        bfree(pB, "VE", vv)
        prog.drain("vector")
        if dst_sb is not None:
            return VE(lambda v: v.tensor_add(dst_sb, tmp1[:, 0:n], tmp2[:, 0:n]))
        stg_wait("vector", "estA")
        vv = VE(lambda v: v.tensor_add(VA["est"][:, 0:n], tmp1[:, 0:n], tmp2[:, 0:n]))
        W_("sync", "VE", vv)
        for hh in range(2):
            prog.dma("sync", dst_dram(hh, c0, n),
                     VA["est"][64 * hh:64 * hh + 64, 0:n], "SestA")
        stg_mark("estA", "SestA")
        return vv

    slot = list(range(9))
    for li in range(cfg.stage if cfg.phase >= 5 else 0):
        oa = cfg.os_a[li]
        of = cfg.os_f[li]
        prog.dma("sync", qscol[:, :], iqs[li], "Dc")
        prog.dma("sync", kscol[:, :], iks[li], "Dc")
        prog.dma("sync", igcol[:, :], ig_c[li], "Dc")
        prog.dma("sync", b1ac[:, :], ib1a[li], "Dc")
        prog.dma("sync", b1gc[:, :], ib1g[li], "Dc")
        vcl = prog.val("Dc")
        prog.dma("gpsimd", ibbf[:, :], ib_c[li], "Dc2")
        prog.dma("gpsimd", igbf[:, :], ig_c[li], "Dc2")
        prog.dma("gpsimd", b2row[:, :], ib2[li], "Dc2")
        prog.dma("gpsimd", VA["tabq"][0], tabsi[0], "Dtab")
        prog.dma("gpsimd", VA["tabq"][1], tabsi[1], "Dtab")
        vtabq = prog.val("Dtab")
        W_("scalar", "Dc", vcl)
        vsc = AC(lambda s: s.mul(qscol[:, :], qscol[:, :], 8.0))

        t_in = max(0, oa - 3)
        build_yT(t_in)
        vyT = prog.val("VE")
        qgr = tile_chunks(oa, TT, 4)
        kgr = tile_chunks(max(0, oa - 2), TT, 4)

        def qkv_mm(pp, wb, g0, n):
            c0 = g0 * 128
            for kc in range(8):
                off = (1 if kc < 4 else 0) + c0
                srcv = yT[slot[kc]]
                vm = MM(lambda t, wb=wb, kc=kc, off=off, n=n, pp=pp, srcv=srcv:
                        t.matmul(pp[:, 0:n], wb[:, kc, :], srcv[:, off:off + n],
                                 start=(kc == 0), stop=(kc == 7)))
            return vm

        W_("vector", "AC", vsc)
        W_("vector", "Dtab", vtabq)
        W_("vector", "Dc2", prog.val("Dc2"))
        for cc in range(8):
            wb = VA["wqs"][cc % 2]
            stg_wait("gpsimd", f"wq{cc % 2}")
            prog.dma("gpsimd", wb,
                     iWqkv[li][:, cc * 128:(cc + 1) * 128]
                     .rearrange("(k p) n -> p k n", p=128), f"Dw{cc % 2}")
            vw = prog.val(f"Dw{cc % 2}")
            W_("tensor", f"Dw{cc % 2}", vw)
            W_("tensor", "VE", vyT)
            for gi, (g0, gn) in enumerate(qgr):
                n = gn * 128
                pp = pA0 if gi % 2 == 0 else pA1
                bwait(pp)
                vmq = qkv_mm(pp, wb, g0, n)
                W_("scalar", "MM", vmq)
                rot_norm_store("q", g0 * 128, n, pp,
                               dst_dram=lambda hh, c0, n, cc=cc:
                               qibuf[2 * cc + hh, :, c0:c0 + n])
            stg_free[f"wq{cc % 2}"] = ("MM", prog.val("MM"))
        W_("gpsimd", "VE", prog.val("VE"))
        prog.dma("gpsimd", VA["tabq"][0], tabsi[2], "Dtab")
        prog.dma("gpsimd", VA["tabq"][1], tabsi[3], "Dtab")
        W_("vector", "Dtab", prog.val("Dtab"))
        for cc in range(8):
            wb = VA["wqs"][cc % 2]
            stg_wait("gpsimd", f"wq{cc % 2}")
            prog.dma("gpsimd", wb,
                     iWqkv[li][:, C + cc * 128:C + (cc + 1) * 128]
                     .rearrange("(k p) n -> p k n", p=128), f"Dw{cc % 2}")
            vw = prog.val(f"Dw{cc % 2}")
            W_("tensor", f"Dw{cc % 2}", vw)
            for gi, (g0, gn) in enumerate(kgr):
                n = gn * 128
                pp = pA0 if gi % 2 == 0 else pA1
                bwait(pp)
                vmq = qkv_mm(pp, wb, g0, n)
                W_("scalar", "MM", vmq)
                rot_norm_store("k", g0 * 128, n, pp,
                               dst_sb=kres[cc][:, g0 * 128:g0 * 128 + n])
            stg_free[f"wq{cc % 2}"] = ("MM", prog.val("MM"))
        if "kidbg" in cfg.debug:
            W_("sync", "VE", prog.val("VE"))
            for cc in range(8):
                prog.dma("sync", kidbg[cc], kres[cc], "Sdbg")
            W_("sync", "Sdbg", prog.val("Sdbg"))
        W_("gpsimd", "MM", prog.val("MM"))
        prog.dma("gpsimd", VA["wvs"][0],
                 iWqkv[li][:, 2 * C:2 * C + 512]
                 .rearrange("(k p) n -> p k n", p=128), "Dwv")
        prog.dma("gpsimd", VA["wvs"][1],
                 iWqkv[li][:, 2 * C + 512:3 * C]
                 .rearrange("(k p) n -> p k n", p=128), "Dwv")
        vwv = prog.val("Dwv")
        W_("tensor", "Dwv", vwv)
        for j in range(max(0, oa - 2), TT):
            vstf = []
            for half in range(2):
                pp = pA0 if half == 0 else pA1
                bwait(pp)
                for kc in range(8):
                    off = (1 if kc < 4 else 0) + j * 128
                    srcv = yT[slot[kc]]
                    vm = MM(lambda t, kc=kc, off=off, pp=pp, half=half, srcv=srcv:
                            t.matmul(pp[:, :], srcv[:, off:off + 128],
                                     VA["wvs"][half][:, kc, :],
                                     start=(kc == 0), stop=(kc == 7)))
                W_("vector", "MM", vm)
                stg_wait("vector", "vstgf")
                vv = VE(lambda v, pp=pp, half=half: v.tensor_copy(
                    VA["vstgf"][:, half * 512:(half + 1) * 512], pp[:, :]))
                bfree(pp, "VE", vv)
                vstf.append(vv)
            W_("sync", "VE", vstf[-1])
            prog.dma("sync", vibuf[j], VA["vstgf"], "Svst")
            stg_mark("vstgf", "Svst")
        prog.barrier()
        stg_free.clear()

        prog.dma("gpsimd", wpro, iWo[li].rearrange("(k p) n -> p k n", p=128), "Dwp")
        vwo = prog.val("Dwp")
        W_("tensor", "Dwp", vwo)
        for i3 in range(3):
            E("vector", lambda v, i3=i3: v.memset(VB["vstg3"][i3][:, :, 64:65], 1.0))
        prog.drain("vector")
        von = prog.val("VE")
        for pre in (oa - 2, oa - 1):
            prog.dma("sync", VB["vstg3"][pre % 3][:, :, 0:64],
                     vibuf[pre].rearrange("p (a b) -> p a b", b=64), f"Dvs{pre % 3}")
        aot_mm_done = [0, 0]
        vstg_mm = [0, 0, 0]
        for j in range(oa, TT):
            W_("sync", "MM", vstg_mm[j % 3])
            prog.dma("sync", VB["vstg3"][j % 3][:, :, 0:64],
                     vibuf[j].rearrange("p (a b) -> p a b", b=64), f"Dvs{j % 3}")
            stg_wait("sync", "qtile")
            prog.dma("sync", VB["qtile"],
                     qibuf[:, :, j * 128:(j + 1) * 128]
                     .rearrange("(hh g) d t -> (g d) hh t", g=2), "Dqt")
            for s3 in range(3):
                W_("tensor", f"Dvs{s3}", prog.val(f"Dvs{s3}"))
            W_("tensor", "Dqt", prog.val("Dqt"))
            if "qtdbg" in cfg.debug and j == 3 and li == 0:
                W_("sync", "Dqt", prog.val("Dqt"))
                prog.dma("sync", qtdbg[0],
                         VB["qtile"].rearrange("p a b -> p (a b)"), "Sdbg")
                W_("sync", "Sdbg", prog.val("Sdbg"))
            ao = VB["aot"][j % 2]
            W_("vector", "MM", aot_mm_done[j % 2])
            if "aodbg" in cfg.debug:
                W_("vector", "Sdbg", prog.val("Sdbg"))
            for h in range(H):
                ch, hi = h // 2, h % 2
                pp = pA0 if h % 2 == 0 else pA1
                bwait(pp)
                k0 = (j - 2) * 128
                if "opdbg" in cfg.debug and j == 3 and h == 2 and li == 0:
                    W_("sync", "Dqt", prog.val("Dqt"))
                    prog.dma("sync", opdbg[0][0:64, 0:128],
                             VB["qtile"][64 * hi:64 * hi + 64, ch, :], "Sdbg")
                    prog.dma("sync", opdbg[1][0:64, 0:384],
                             kres[ch][64 * hi:64 * hi + 64, k0:k0 + 384], "Sdbg")
                MM(lambda t, ch=ch, hi=hi, pp=pp, k0=k0: t.matmul(
                    pp[:, 0:384], VB["qtile"][64 * hi:64 * hi + 64, ch, :],
                    kres[ch][64 * hi:64 * hi + 64, k0:k0 + 384],
                    start=True, stop=False))
                vm = MM(lambda t, pp=pp, k0=k0: t.matmul(
                    pp[:, 0:384], onesrow[0:1, 0:128], invt[0:1, k0:k0 + 384],
                    start=False, stop=True))
                W_("vector", "MM", vm)
                vv = VE(lambda v, pp=pp: v.tensor_add(s2f[:, :], pp[:, 0:384],
                                                      trit[:, :]))
                bfree(pp, "VE", vv)
                prog.drain("vector")
                if "s2dbg" in cfg.debug and j == 3 and h == 2 and li == 0:
                    W_("sync", "VE", prog.val("VE"))
                    prog.dma("sync", s2dbg[0][:, 0:384], s2f[:, :], "Sdbg")
                    W_("sync", "Sdbg", prog.val("Sdbg"))
                    W_("vector", "Sdbg", prog.val("Sdbg"))
                vv = VE(lambda v: v.reduce_max(nmx[:, :], s2f[:, :], AX.X,
                                               negate=True))
                W_("scalar", "VE", vv)
                stg_wait("scalar", "pbf")
                if "pdbg" in cfg.debug:
                    W_("scalar", "Sdbg", prog.val("Sdbg"))
                va = AC(lambda s: s.activation(VB["pbf"], s2f[:, :], AF.Exp,
                                               bias=nmx[:, :]))
                W_("tensor", "AC", va)
                for kt in range(3):
                    transpose1(VB["pbf"][:, kt * 128:(kt + 1) * 128],
                               VB["ptsb"][:, kt * 128:(kt + 1) * 128], 4 + kt % 2)
                stg_free["pbf"] = ("MM", prog.val("MM"))
                if "pdbg" in cfg.debug and j == 3 and h == 2 and li == 0:
                    W_("sync", "AC", prog.val("AC"))
                    prog.dma("sync", pdbg[0][:, 0:384], VB["pbf"], "Sdbg")
                    vsd = VE(lambda v: v.tensor_copy(VB["ptsb"][:, 0:1],
                                                     VB["ptsb"][:, 0:1]))
                vtr = prog.val("VE")
                bwait(pO)
                W_("tensor", "VE", vtr)
                if h == 0:
                    W_("tensor", "VE", von)
                for kt in range(3):
                    vm = MM(lambda t, kt=kt, j=j, h=h: t.matmul(
                        pO[:, 0:65], VB["ptsb"][:, kt * 128:(kt + 1) * 128],
                        VB["vstg3"][(j - 2 + kt) % 3][:, h, :],
                        start=(kt == 0), stop=(kt == 2)))
                    vstg_mm[(j - 2 + kt) % 3] = vm
                if "pdbg" in cfg.debug and j == 3 and h == 2 and li == 0:
                    W_("sync", "MM", vm)
                    W_("sync", "VE", prog.val("VE"))
                    prog.dma("sync", pdbg[1][:, 0:384], VB["ptsb"], "Sdbg")
                W_("vector", "MM", vm)
                prog.drain("vector")
                E("vector", lambda v: v.reciprocal(rc1[:, :], pO[:, 64:65]))
                prog.drain("vector")
                vv = VE(lambda v, ao=ao, h=h: v.tensor_scalar(
                    ao[:, h * 64:(h + 1) * 64], pO[:, 0:64], rc1[:, :], None, ALU.mult))
                bfree(pO, "VE", vv)
            stg_free["qtile"] = ("MM", prog.val("MM"))
            vao = prog.val("VE")
            if "aodbg" in cfg.debug:
                W_("sync", "VE", vao)
                prog.dma("sync", aodbg[j], ao, "Sdbg")
            W_("tensor", "VE", vao)
            for chh in range(8):
                transpose1(ao[:, chh * 128:(chh + 1) * 128],
                           VB["aofm"][:, chh * 128:(chh + 1) * 128], chh % 4)
            aot_mm_done[j % 2] = prog.val("MM")
            vtr = prog.val("VE")
            W_("tensor", "VE", vtr)
            bwait(pB)
            for oc in range(2):
                for chh in range(8):
                    vm = MM(lambda t, chh=chh, oc=oc: t.matmul(
                        pB[:, oc * 512:(oc + 1) * 512],
                        VB["aofm"][:, chh * 128:(chh + 1) * 128],
                        wpro[:, chh, oc * 512:(oc + 1) * 512],
                        start=(chh == 0), stop=(chh == 7)))
            zi = ztile[j % 2]
            stg_wait("sync", f"zt{j % 2}")
            vput = prog.dma("sync", zi[:, :], zbuf[j], f"Dzt{j % 2}")
            W_("vector", f"Dzt{j % 2}", vput)
            W_("vector", "MM", vm)
            E("vector", lambda v, zi=zi: v.tensor_add(zi[:, 0:512], zi[:, 0:512],
                                                      pB[:, 0:512]))
            vv = VE(lambda v, zi=zi: v.tensor_add(zi[:, 512:1024], zi[:, 512:1024],
                                                  pB[:, 512:1024]))
            bfree(pB, "VE", vv)
            W_("sync", "VE", vv)
            prog.dma("sync", zbuf[j], zi[:, :], f"Szt{j % 2}")
            stg_free[f"zt{j % 2}"] = (f"Szt{j % 2}", prog.val(f"Szt{j % 2}"))
        prog.barrier()
        stg_free.clear()

        if cfg.half and li == cfg.stage - 1:
            break
        for kc in range(NJ2):
            r0 = kc * 128
            nr = min(128, DI - r0)
            prog.dma("gpsimd", w2res[kc][0:nr, :], iW2[li][r0:r0 + nr, :], "Dw2")
        vw2 = prog.val("Dw2")
        build_yT(max(0, of - 1))
        vyT = prog.val("VE")
        fgr = tile_chunks(of, TT, 4)
        W_("vector", "Dc", vcl)
        W_("vector", "Dc2", prog.val("Dc2"))
        for (g0, gn) in fgr:
            n = gn * 128
            c0 = g0 * 128
            bwait(pL)
            bwait(pR)
            W_("tensor", "VE", vyT)
            W_("scalar", "VE", vyT)
            for kc in range(8):
                off = (1 if kc < 4 else 0) + c0
                srcv = yT[slot[kc]]
                MM(lambda t, kc=kc, off=off, n=n, srcv=srcv: t.matmul(
                    pR[0:1, 0:n], onescol[:, :], srcv[:, off:off + n],
                    start=(kc == 0), stop=(kc == 7)))
            lastm = None
            for kc in range(8):
                off = (1 if kc < 4 else 0) + c0
                stg_wait("scalar", "fsqb")
                srcv = yT[slot[kc]]
                va = AC(lambda s, kc=kc, off=off, n=n, srcv=srcv: s.activation(
                    VF["sqb"][:, 0:n], srcv[:, off:off + n], AF.Square))
                W_("tensor", "AC", va)
                lastm = MM(lambda t, kc=kc, n=n: t.matmul(
                    pL[0:1, 0:n], onescol[:, :], VF["sqb"][:, 0:n],
                    start=(kc == 0), stop=(kc == 7)))
                stg_free["fsqb"] = ("MM", lastm)
            W_("vector", "MM", lastm)
            E("vector", lambda v, n=n: v.tensor_scalar(rowsA[0:1, 0:n], pR[0:1, 0:n],
                                                       1.0 / C, None, ALU.mult))
            vv = VE(lambda v, n=n: v.tensor_scalar(rowsB[0:1, 0:n], pL[0:1, 0:n],
                                                   1.0 / C, None, ALU.mult))
            bfree(pR, "VE", vv)
            bfree(pL, "VE", vv)
            prog.drain("vector")
            E("vector", lambda v, n=n: v.tensor_mul(rowsC[0:1, 0:n], rowsA[0:1, 0:n],
                                                    rowsA[0:1, 0:n]))
            prog.drain("vector")
            vv = VE(lambda v, n=n: v.tensor_sub(rowsB[0:1, 0:n], rowsB[0:1, 0:n],
                                                rowsC[0:1, 0:n]))
            W_("scalar", "VE", vv)
            E("scalar", lambda s, n=n: s.activation(rowsB[0:1, 0:n], rowsB[0:1, 0:n],
                                                    AF.Ln, bias=epst[0:1, :]))
            prog.drain("scalar")
            va = AC(lambda s, n=n: s.activation(rowsB[0:1, 0:n], rowsB[0:1, 0:n],
                                                AF.Exp, scale=-0.5))
            W_("vector", "AC", va)
            VE(lambda v, c0=c0, n=n: v.tensor_copy(VF["rstdb"][0:1, c0:c0 + n],
                                                   rowsB[0:1, 0:n]))
            prog.drain("vector")
            E("vector", lambda v, n=n: v.tensor_mul(rowsC[0:1, 0:n], rowsA[0:1, 0:n],
                                                    rowsB[0:1, 0:n]))
            prog.drain("vector")
            VE(lambda v, c0=c0, n=n: v.tensor_scalar(
                VF["nmrst"][0:1, c0:c0 + n], rowsC[0:1, 0:n], -1.0, None, ALU.mult))
        vstats = prog.val("VE")
        spare = slot[8]
        for ch in range(8):
            dst = spare
            off = 1 if ch < 4 else 0
            for (g0, gn) in fgr:
                n = gn * 128
                c0 = g0 * 128
                bwait(pL)
                W_("tensor", "VE", vstats)
                vm = MM(lambda t, c0=c0, n=n: t.matmul(
                    pL[:, 0:n], onesrow[0:1, 0:128], VF["rstdb"][0:1, c0:c0 + n],
                    start=True, stop=True))
                W_("vector", "MM", vm)
                dstv, srcv = yT[dst], yT[slot[ch]]
                vv = VE(lambda v, ch=ch, c0=c0, n=n, off=off, dstv=dstv, srcv=srcv:
                        v.scalar_tensor_tensor(
                            dstv[:, c0:c0 + n],
                            srcv[:, off + c0:off + c0 + n],
                            igcol[:, ch:ch + 1], pL[:, 0:n], ALU.mult, ALU.mult))
                bfree(pL, "VE", vv)
            spare = slot[ch]
            slot[ch] = dst
        vxbar = prog.val("VE")
        slot[8] = spare
        for pr in range(NJ2):
            na = min(128, DI - pr * 128)
            wb = VF["w1s"][pr % 2]
            stg_wait("gpsimd", f"wq{pr % 2}")
            prog.dma("gpsimd", wb[:, :, 0:na],
                     iW1[li][:, pr * 128:pr * 128 + na]
                     .rearrange("(k p) n -> p k n", p=128), f"Dw{pr % 2}")
            prog.dma("gpsimd", wb[:, :, 128:128 + na],
                     iW1[li][:, DI + pr * 128:DI + pr * 128 + na]
                     .rearrange("(k p) n -> p k n", p=128), f"Dw{pr % 2}")
            vw = prog.val(f"Dw{pr % 2}")
            W_("tensor", f"Dw{pr % 2}", vw)
            W_("tensor", "VE", vxbar)
            bwait(pR)
            for side in range(2):
                for kc in range(8):
                    vm = MM(lambda t, kc=kc, side=side, na=na, pr=pr: t.matmul(
                        pR[0:1, side * 128:side * 128 + na], igbf[:, kc:kc + 1],
                        VF["w1s"][pr % 2][:, kc, side * 128:side * 128 + na],
                        start=(kc == 0), stop=(kc == 7)))
            W_("vector", "MM", vm)
            stg_wait("vector", "browa")
            vbr = VE(lambda v, na=na: v.tensor_copy(VF["browa"][0:1, 0:256],
                                                    pR[0:1, 0:256]))
            bfree(pR, "VE", vbr)
            bwait(pO)
            for side in range(2):
                for kc in range(8):
                    vm = MM(lambda t, kc=kc, side=side, na=na, pr=pr: t.matmul(
                        pO[0:na, side:side + 1], VF["w1s"][pr % 2][:, kc,
                                                                  side * 128:side * 128 + na],
                        ibbf[:, kc:kc + 1], start=(kc == 0), stop=(kc == 7)))
            W_("vector", "MM", vm)
            prog.drain("vector")
            E("vector", lambda v, na=na, pr=pr: v.tensor_add(
                biasa[0:na, :], pO[0:na, 0:1], b1ac[0:na, pr:pr + 1]))
            vbc = VE(lambda v, na=na, pr=pr: v.tensor_add(
                biasg[0:na, :], pO[0:na, 1:2], b1gc[0:na, pr:pr + 1]))
            bfree(pO, "VE", vbc)
            prog.drain("vector")
            W_("tensor", "VE", vbr)
            for gi, (g0, gn) in enumerate(fgr):
                n = gn * 128
                c0 = g0 * 128
                pair = (pA0, pA1) if gi % 2 == 0 else (pB[:, 0:512], pB[:, 512:1024])
                key = ("pa", gi % 2)
                if key in bank_free:
                    s_, c_ = bank_free.pop(key)
                    W_("tensor", s_, c_)
                for side, ppx in enumerate(pair):
                    for kc in range(8):
                        srcv = yT[slot[kc]]
                        MM(lambda t, kc=kc, side=side, na=na, n=n, c0=c0, ppx=ppx,
                           pr=pr, srcv=srcv:
                           t.matmul(ppx[0:na, 0:n],
                                    VF["w1s"][pr % 2][:, kc, side * 128:side * 128 + na],
                                    srcv[:, c0:c0 + n],
                                    start=(kc == 0), stop=False))
                    vm = MM(lambda t, side=side, na=na, n=n, c0=c0, ppx=ppx:
                            t.matmul(ppx[0:na, 0:n],
                                     VF["browa"][0:1, side * 128:side * 128 + na],
                                     VF["nmrst"][0:1, c0:c0 + n],
                                     start=False, stop=True))
                W_("scalar", "MM", vm)
                W_("scalar", "VE", vbc)
                stg_wait("scalar", "fgg")
                va = AC(lambda s, pair=pair, na=na, n=n: s.activation(
                    VF["gg"][0:na, 0:n], pair[1][0:na, 0:n], AF.Gelu,
                    bias=biasg[0:na, :]))
                W_("vector", "AC", va)
                stg_wait("vector", "fzstg")
                vv = VE(lambda v, pair=pair, na=na, n=n: v.scalar_tensor_tensor(
                    VF["zstg"][0:na, 0:n], pair[0][0:na, 0:n], biasa[0:na, :],
                    VF["gg"][0:na, 0:n], ALU.add, ALU.mult))
                stg_free["fgg"] = ("VE", vv)
                bank_free[("pa", gi % 2)] = ("VE", vv)
                W_("sync", "VE", vv)
                prog.dma("sync", zffn[pr][0:na, c0:c0 + n], VF["zstg"][0:na, 0:n], "Szs")
                stg_mark("fzstg", "Szs")
            stg_free[f"wq{pr % 2}"] = ("MM", prog.val("MM"))
        bank_free.pop(("pa", 0), None)
        bank_free.pop(("pa", 1), None)
        prog.barrier()
        stg_free.clear()

        W_("tensor", "Dw2", vw2)
        w2gr = tile_chunks(of, TT, 2)
        for (g0, gn) in w2gr:
            pslots = [(pA0, pA1), (pB[:, 0:512], pB[:, 512:1024])]
            for ti in range(gn):
                for ppx in pslots[ti]:
                    bwait(ppx)
            zs_mm = [0, 0]
            for kc in range(NJ2):
                nr = min(128, DI - kc * 128)
                zz = VW["zsrm"][kc % 2]
                W_("sync", "MM", zs_mm[kc % 2])
                prog.dma("sync", zz[0:nr, 0:gn * 128],
                         zffn[kc][0:nr, g0 * 128:g0 * 128 + gn * 128], f"Dzs{kc % 2}")
                W_("tensor", f"Dzs{kc % 2}", prog.val(f"Dzs{kc % 2}"))
                for ti in range(gn):
                    for oc in range(2):
                        vm = MM(lambda t, kc=kc, nr=nr, ti=ti, oc=oc, g0=g0:
                                t.matmul(pslots[ti][oc][:, :],
                                         VW["zsrm"][kc % 2][0:nr, ti * 128:(ti + 1) * 128],
                                         w2res[kc][0:nr, oc * 512:(oc + 1) * 512],
                                         start=(kc == 0), stop=False))
                zs_mm[kc % 2] = vm
            for ti in range(gn):
                for oc in range(2):
                    vm = MM(lambda t, ti=ti, oc=oc: t.matmul(
                        pslots[ti][oc][:, :], onesrow[0:1, 0:128],
                        b2row[0:1, oc * 512:(oc + 1) * 512], start=False, stop=True))
            W_("vector", "MM", vm)
            for ti in range(gn):
                j = g0 + ti
                zi = ztile[ti % 2]
                stg_wait("sync", f"zt{ti % 2}")
                vput = prog.dma("sync", zi[:, :], zbuf[j], f"Dzt{ti % 2}")
                W_("vector", f"Dzt{ti % 2}", vput)
                E("vector", lambda v, zi=zi, ti=ti: v.tensor_add(
                    zi[:, 0:512], zi[:, 0:512], pslots[ti][0][:, :]))
                vv = VE(lambda v, zi=zi, ti=ti: v.tensor_add(
                    zi[:, 512:1024], zi[:, 512:1024], pslots[ti][1][:, :]))
                bfree(pslots[ti][0], "VE", vv)
                bfree(pslots[ti][1], "VE", vv)
                W_("sync", "VE", vv)
                prog.dma("sync", zbuf[j], zi[:, :], f"Szt{ti % 2}")
                stg_free[f"zt{ti % 2}"] = (f"Szt{ti % 2}", prog.val(f"Szt{ti % 2}"))
        prog.barrier()
        stg_free.clear()

    for tt in range(OT):
        zi = ztile[tt % 2]
        stg_wait("sync", f"zt{tt % 2}")
        prog.dma("sync", zi[:, :], zbuf[ET + tt], f"Dzt{tt % 2}")
        stg_wait("sync", "fza")
        vput = prog.dma("sync", ztA[:, :], y1buf[tt], "Dfz")
        W_("vector", f"Dzt{tt % 2}", prog.val(f"Dzt{tt % 2}"))
        W_("vector", "Dfz", vput)
        vv = VE(lambda v, zi=zi: v.tensor_add(zi[:, :], zi[:, :], ztA[:, :]))
        stg_free["fza"] = ("VE", vv)
        W_("sync", "VE", vv)
        prog.dma("sync", out[tt], zi[:, :], f"Szt{tt % 2}")
        stg_free[f"zt{tt % 2}"] = (f"Szt{tt % 2}", prog.val(f"Szt{tt % 2}"))
    for n_, ap_ in dbg_outs.items():
        src_ = scr[n_]
        flat = src_.rearrange("a p c -> (a p) c") if len(src_.shape) == 3 else src_
        prog.dma("sync", ap_, flat, "Sdbg")
    for snm in ("Szt0", "Szt1", "Sdbg", "Sao", "Sest", "Sest2", "SestA",
                "Snb", "Svst", "Szs"):
        W_("sync", snm, prog.val(snm))
    prog.barrier()

    with nc.Block() as block:
        for e in ENGINES:
            ops = prog.ops[e]
            def run(eng, ops=ops):
                for f in ops:
                    f(eng)
            getattr(block, e)(run)
    for g in prog._g:
        g.__exit__(None, None, None)
    return nc



def _prep_inputs(cfg, inputs):
    C, H, HD = cfg.C, cfg.H, cfg.HD
    L = cfg.n_layers
    f32 = np.float32
    x = np.asarray(inputs["L0_attn_x"], dtype=f32)
    w_qkv = np.ascontiguousarray(inputs["L0_attn_w_qkv"], dtype=f32)
    b_qkv = np.asarray(inputs["L0_attn_b_qkv"], dtype=f32)
    bqkv_c = np.ascontiguousarray(b_qkv.reshape(24, 128).T)
    wproj = np.ascontiguousarray(inputs["L0_attn_w_proj"], dtype=f32)
    bproj = np.asarray(inputs["L0_attn_b_proj"], dtype=f32).reshape(1, C)
    iWqkv = np.ascontiguousarray(inputs["L0_ffn_Wqkv"], dtype=f32)
    qs = np.asarray(inputs["L0_ffn_q_scale"], dtype=f32)
    ks = np.asarray(inputs["L0_ffn_k_scale"], dtype=f32)
    iqs = np.ascontiguousarray(np.tile(qs, (1, 2))[:, :, None])
    iks = np.ascontiguousarray(np.tile(ks, (1, 2))[:, :, None])
    iWo = np.ascontiguousarray(inputs["L0_ffn_Wo"], dtype=f32)
    lg = np.asarray(inputs["L0_ffn_ln_g"], dtype=f32)
    lb = np.asarray(inputs["L0_ffn_ln_b"], dtype=f32)
    ig_c = np.ascontiguousarray(lg.reshape(L, 8, 128).transpose(0, 2, 1))
    ib_c = np.ascontiguousarray(lb.reshape(L, 8, 128).transpose(0, 2, 1))
    iW1 = np.ascontiguousarray(inputs["L0_ffn_W1"], dtype=f32)
    b1 = np.asarray(inputs["L0_ffn_b1"], dtype=f32)
    DI = cfg.DI
    b1a = np.zeros((L, 128, 22), dtype=f32)
    b1g = np.zeros((L, 128, 22), dtype=f32)
    for p in range(22):
        na = min(128, DI - p * 128)
        b1a[:, 0:na, p] = b1[:, p * 128:p * 128 + na]
        b1g[:, 0:na, p] = b1[:, DI + p * 128:DI + p * 128 + na]
    iW2 = np.ascontiguousarray(inputs["L0_ffn_W2"], dtype=f32)
    ib2 = np.ascontiguousarray(inputs["L0_ffn_b2"], dtype=f32)[:, None, :]
    ident = np.eye(128, dtype=BF)
    p2 = host_rotperm()
    tri = host_trimask(cfg)
    in_maps = []
    for core in range(8):
        b, q = core // 4, core % 4
        own0 = q * cfg.OWN
        xb = x[b]
        xt = np.zeros((cfg.TOT, C), dtype=f32)
        lo = own0 - cfg.EXT
        src_lo = max(0, lo)
        xt[src_lo - lo:cfg.TOT - max(0, (own0 + cfg.OWN) - cfg.NB)] = \
            xb[src_lo:own0 + cfg.OWN]
        tabs = host_rotary(cfg, core)
        inv = np.zeros((1, cfg.TOT), dtype=BF)
        ninval = max(0, -lo)
        if ninval > 0:
            inv[0, 0:ninval] = BF(NEG)
        zx = np.full((128, 1), 0.0 if q == 0 else 1.0, dtype=f32)
        in_maps.append({
            "x_tot": np.ascontiguousarray(xt.reshape(cfg.TT, 128, C)),
            "x_full": np.ascontiguousarray(xb),
            "w_qkv": w_qkv, "bqkv_c": bqkv_c,
            "a_q": np.ascontiguousarray(inputs["L0_attn_a_q"], dtype=f32),
            "b_q": np.ascontiguousarray(inputs["L0_attn_b_q"], dtype=f32),
            "a_v": np.ascontiguousarray(inputs["L0_attn_a_v"], dtype=f32),
            "b_v": np.ascontiguousarray(inputs["L0_attn_b_v"], dtype=f32),
            "w_proj": wproj, "bproj_r": bproj,
            "iWqkv": iWqkv, "iqs": iqs, "iks": iks, "iWo": iWo,
            "ig_c": ig_c, "ib_c": ib_c, "iW1": iW1, "ib1a": b1a, "ib1g": b1g,
            "iW2": iW2, "ib2": ib2,
            "identi": ident, "p2i": p2, "tabsi": tabs, "trii": tri,
            "invi": inv, "zxi": zx,
        })
    return in_maps


_CACHED = {}


def kernel(**inputs):
    cfg = _CACHED.get("cfg")
    if cfg is None:
        cfg = Cfg()
        _CACHED["cfg"] = cfg
    nc = _CACHED.get("nc")
    if nc is None:
        nc = build_program(cfg)
        _CACHED["nc"] = nc
    in_maps = _prep_inputs(cfg, inputs)
    res = run_bass_kernel_spmd(nc, in_maps, core_ids=list(range(8)))
    B, NB, C = 2, cfg.NB, cfg.C
    outp = np.zeros((B, NB, C), dtype=np.float32)
    for core in range(8):
        b, q = core // 4, core % 4
        outp[b, q * cfg.OWN:(q + 1) * cfg.OWN] = \
            res.results[core]["out"].reshape(cfg.OWN, C)
    return outp
